# version 04
# baseline (speedup 1.0000x reference)
import contextlib
import math
import os
import numpy as np
import ml_dtypes
import concourse.bass as bass
import concourse.mybir as mybir
from concourse.bass_utils import run_bass_kernel_spmd

F32 = mybir.dt.float32
BF16 = mybir.dt.bfloat16
AF = mybir.ActivationFunctionType
ALU = mybir.AluOpType
AX = mybir.AxisListType

T = 4096
D = 1024
NT = T // 128
ND = D // 128
H = 16
DH = 64
E = 16
BF = ml_dtypes.bfloat16
ENGS = ("q", "s", "v", "t", "g")


def _alibi_slopes(n_heads):
    def pow2(n):
        start = 2 ** (-2 ** (-(math.log2(n) - 3)))
        return [start ** i for i in range(n)]
    assert math.log2(n_heads).is_integer()
    return np.asarray(pow2(n_heads), dtype=np.float32)


def host_constants():
    c = {}
    c["c_ident"] = np.eye(128, dtype=np.float32).astype(BF)
    i = np.arange(128)
    same = (i[:, None] // 64) == (i[None, :] // 64)
    causal = i[None, :] <= i[:, None]
    c["c_mask0"] = np.where(same & causal, 0.0, -1e9).astype(np.float32)
    low = i[None, :] < i[:, None]
    c["c_lmask"] = low.astype(np.float32)
    c["c_umask"] = low.T.copy().astype(np.float32)
    tok = np.arange(T)
    bits = np.stack([(((tok >> b) & 1) << b) for b in range(12)]).astype(np.float32)
    c["c_qaug"] = np.concatenate([bits, np.ones((12, T), np.float32)]).astype(BF)
    ones12 = np.ones((12, T), np.float32)
    c["c_klow"] = np.concatenate([-ones12, bits]).astype(BF)
    c["c_khigh"] = np.concatenate([ones12, -bits]).astype(BF)
    pj = np.arange(128)
    c["c_corr"] = (2.0 * np.maximum(pj[:, None] - pj[None, :], 0)).astype(np.float32)
    c["c_slope8"] = (8.0 * _alibi_slopes(H))[None, :].astype(np.float32)
    ohh = np.zeros((E, E * 128), np.float32)
    for e in range(E):
        ohh[e, e * 128:(e + 1) * 128] = 1.0
    c["c_oh"] = ohh
    c["c_identf"] = np.eye(128, dtype=np.float32)
    return c


def host_prep(inputs):
    ap = {}
    ap["wqT"] = np.ascontiguousarray(np.asarray(inputs["L0_attn_Wq"]).T)
    ap["wkT"] = np.ascontiguousarray(np.asarray(inputs["L0_attn_Wk"]).T)
    ap["wvT"] = np.ascontiguousarray(np.asarray(inputs["L0_attn_Wv"]).T)
    ap["woT"] = np.ascontiguousarray(np.asarray(inputs["L0_attn_Wo"]).T)
    ap["alpha_raw"] = np.ascontiguousarray(np.asarray(inputs["L0_attn_alpha_raw"]).reshape(1, H))
    ap["beta_raw"] = np.ascontiguousarray(np.asarray(inputs["L0_attn_beta_raw"]).reshape(1, H))
    ap["esT"] = np.ascontiguousarray(np.asarray(inputs["L0_ffn_expert_sel"]).T)
    ap["keys"] = np.ascontiguousarray(np.asarray(inputs["L0_ffn_keys"]).reshape(E * D, 128))
    ap["values"] = np.ascontiguousarray(np.asarray(inputs["L0_ffn_values"]).reshape(E * 128, D))
    ap["wqkv"] = np.ascontiguousarray(np.asarray(inputs["L1_attn_Wqkv"]))
    ap["bqkv"] = np.ascontiguousarray(np.asarray(inputs["L1_attn_bqkv"]).reshape(24, 128).T)
    ap["wout"] = np.ascontiguousarray(np.asarray(inputs["L1_attn_Wout"]))
    ap["bout"] = np.ascontiguousarray(np.asarray(inputs["L1_attn_bout"]).reshape(1, D))
    ap["gwT"] = np.ascontiguousarray(np.asarray(inputs["L1_ffn_gate_w"]).T)
    ap["w1T"] = np.ascontiguousarray(
        np.asarray(inputs["L1_ffn_w1"]).transpose(0, 2, 1).reshape(E * D, 512))
    ap["w2T"] = np.ascontiguousarray(
        np.asarray(inputs["L1_ffn_w2"]).transpose(0, 2, 1).reshape(E * 512, D))
    ap["sfc1T"] = np.ascontiguousarray(np.asarray(inputs["L1_ffn_sfc1"]).T)
    ap["sfc2T"] = np.ascontiguousarray(np.asarray(inputs["L1_ffn_sfc2"]).T)
    ap["sfc3T"] = np.ascontiguousarray(np.asarray(inputs["L1_ffn_sfc3"]).T)
    return ap


WEIGHT_SHAPES = {
    "wqT": ([D, D], F32), "wkT": ([D, D], F32), "wvT": ([D, D], F32),
    "woT": ([D, D], F32),
    "alpha_raw": ([1, H], F32), "beta_raw": ([1, H], F32),
    "esT": ([D, E], F32), "keys": ([E * D, 128], F32), "values": ([E * 128, D], F32),
    "wqkv": ([D, 3 * D], F32), "bqkv": ([128, 24], F32),
    "wout": ([D, D], F32), "bout": ([1, D], F32),
    "gwT": ([D, E], F32), "w1T": ([E * D, 512], F32), "w2T": ([E * 512, D], F32),
    "sfc1T": ([D, 2048], F32), "sfc2T": ([D, 2048], F32), "sfc3T": ([2048, D], F32),
}
CONST_SHAPES = {
    "c_ident": ([128, 128], BF16), "c_mask0": ([128, 128], F32),
    "c_lmask": ([128, 128], F32), "c_umask": ([128, 128], F32),
    "c_qaug": ([24, T], BF16), "c_klow": ([24, T], BF16), "c_khigh": ([24, T], BF16),
    "c_corr": ([128, 128], F32), "c_slope8": ([1, H], F32),
    "c_oh": ([E, E * 128], F32), "c_identf": ([128, 128], F32),
}
INTER_SHAPES = {
    "xn0": ([D, T], BF16),
    "q0": ([D, T], BF16), "k0": ([D, T], BF16), "v0": ([D, T], BF16),
    "mo0": ([D, T], BF16),
    "s0": ([T, D], F32),
    "x1": ([T, D], F32),
    "xn1": ([D, T], BF16),
    "mo1": ([D, T], BF16),
    "s1": ([T, D], F32),
    "shared1": ([T, D], F32),
    "disp": ([E, T], F32),
    "hmoe": ([E * 512, T], BF16),
}


class Tok:
    __slots__ = ("sem", "n")

    def __init__(self, sem, n):
        self.sem, self.n = sem, n


class SemPool:

    def __init__(self, nc, ES, n=40):
        self.nc, self.ES = nc, ES
        self.sems = [ES.enter_context(nc.semaphore(f"sp{i}")) for i in range(n)]
        self.vals = [0] * n
        self.idx = 0
        self.leased = []
        self.ndecl = 0

    def dedicated(self, name):
        self.ndecl += 1
        return self.ES.enter_context(self.nc.semaphore(f"dd{self.ndecl}_{name}"))

    def take(self):
        i = self.idx % len(self.sems)
        self.idx += 1
        self.leased.append(i)
        return i

    def settle(self, i, final):
        self.vals[i] = final


class Ctx:

    def __init__(self, nc, pool, tag):
        self.nc, self.tag, self.pool = nc, tag, pool
        self.sidx = {e: pool.take() for e in ENGS}
        self.sems = {e: pool.sems[self.sidx[e]] for e in ENGS}
        self.cnt = {e: pool.vals[self.sidx[e]] for e in ENGS}
        self.base = dict(self.cnt)
        self.gidx = {}
        self.gsems = {}
        self.gcnt = {}
        self.gowner = {}
        self.queues = {e: [] for e in ENGS}

    def op(self, e, fn, after=(), inc=1, sem=None):
        for tk in after:
            if tk is not None:
                self.queues[e].append(("w", tk))
        tok = None
        if inc:
            if sem is None:
                self.cnt[e] += inc
                tok = Tok(self.sems[e], self.cnt[e])
                semh = self.sems[e]
            else:
                if sem not in self.gsems:
                    if e == "g":
                        self.gidx[sem] = None
                        self.gsems[sem] = self.pool.dedicated(f"{self.tag}_{sem}")
                        self.gcnt[sem] = 0
                    else:
                        gi = self.pool.take()
                        self.gidx[sem] = gi
                        self.gsems[sem] = self.pool.sems[gi]
                        self.gcnt[sem] = self.pool.vals[gi]
                    self.gowner[sem] = e
                self.gcnt[sem] += inc
                tok = Tok(self.gsems[sem], self.gcnt[sem])
                semh = self.gsems[sem]
            self.queues[e].append(("o", fn, inc, semh))
        else:
            self.queues[e].append(("o", fn, 0, None))
        return tok

    def replay(self, bar):
        nc = self.nc
        prev_n = bar["n"]
        gfin = {}
        for gname, owner in self.gowner.items():
            gfin.setdefault(owner, []).append((self.gsems[gname], self.gcnt[gname]))
        with nc.Block() as blk:
            for e in ENGS:
                def mk(eng, e=e, queue=self.queues[e], total=self.cnt[e]):
                    if prev_n > 0:
                        eng.wait_ge(bar["sem"], prev_n)
                    water = {}
                    for item in queue:
                        if item[0] == "w":
                            tk = item[1]
                            key = id(tk.sem)
                            if water.get(key, 0) >= tk.n:
                                continue
                            water[key] = tk.n
                            eng.wait_ge(tk.sem, tk.n)
                        else:
                            _, fn, inc, semh = item
                            inst = fn(eng)
                            if inc:
                                inst.then_inc(semh, inc)
                    if total > self.base[e]:
                        eng.wait_ge(self.sems[e], total)
                    for gs, gn in gfin.get(e, []):
                        eng.wait_ge(gs, gn)
                    eng.sem_inc(bar["sem"], 1)

                getattr(blk, {"q": "sync", "s": "scalar", "v": "vector",
                              "t": "tensor", "g": "gpsimd"}[e])(mk)
        bar["n"] += 5
        for e in ENGS:
            self.pool.settle(self.sidx[e], self.cnt[e])
        for gname, gi in self.gidx.items():
            if gi is not None:
                self.pool.settle(gi, self.gcnt[gname])


def st_setup(nc, ES, bar, dr):
    c = Ctx(nc, _POOL, "su")
    for nm in ("c_ident", "c_mask0", "c_lmask", "c_umask", "c_corr", "c_oh", "c_identf"):
        shp, dt = CONST_SHAPES[nm]
        dr[nm + "_sb"] = ES.enter_context(nc.sbuf_tensor("sb_" + nm, shp, dt))
        c.op("q", lambda q, nm=nm: q.dma_start(out=dr[nm + "_sb"][:], in_=dr[nm]), inc=16, sem="cl")
    c.replay(bar)


def st_ln(nc, ES, bar, dr, x_name, xn_name, tag):
    c = Ctx(nc, _POOL, tag)
    x_ap, xn_ap = dr[x_name], dr[xn_name]
    ident = dr["c_ident_sb"]
    SES = contextlib.ExitStack()
    A = SES.enter_context
    sx = [A(nc.sbuf_tensor(f"{tag}_sx{i}", [128, D], F32)) for i in range(2)]
    sjunk = A(nc.sbuf_tensor(f"{tag}_junk", [128, D], BF16))
    xnt = [A(nc.sbuf_tensor(f"{tag}_xnt{i}", [128, D], BF16)) for i in range(2)]
    sxn = [A(nc.sbuf_tensor(f"{tag}_sxn{i}", [128, D], BF16)) for i in range(2)]
    ssq = A(nc.sbuf_tensor(f"{tag}_ssq", [128, 1], F32))
    ssum = A(nc.sbuf_tensor(f"{tag}_ssum", [128, 1], F32))
    smu = A(nc.sbuf_tensor(f"{tag}_smu", [128, 1], F32))
    svar = A(nc.sbuf_tensor(f"{tag}_svar", [128, 1], F32))
    srstd = A(nc.sbuf_tensor(f"{tag}_srstd", [128, 1], F32))
    snmr = A(nc.sbuf_tensor(f"{tag}_snmr", [128, 1], F32))
    seps = A(nc.sbuf_tensor(f"{tag}_seps", [128, 1], F32))
    ptr = [A(nc.psum_tensor(f"{tag}_ptr{i}", [128, D], BF16)) for i in range(2)]

    t_eps = c.op("v", lambda v: v.memset(seps[:], 1e-5))
    prev = {}
    for i in range(NT):
        b = i % 2
        t_ld = c.op("q", lambda q, i=i, b=b: q.dma_start(
            out=sx[b][:], in_=x_ap[i * 128:(i + 1) * 128, :]),
            after=[prev.get(("apply", b))], inc=16, sem=f"ld{b}")
        t_sq = c.op("s", lambda s, b=b: s.activation(
            sjunk[:], sx[b][:], AF.Square, accum_out=ssq[:]),
            after=[t_ld, prev.get("varuse")])
        t_sum = c.op("v", lambda v, b=b: v.reduce_sum(ssum[:], sx[b][:], axis=AX.X),
                     after=[t_ld, prev.get("var2")])
        t_mu = c.op("v", lambda v: v.tensor_scalar_mul(smu[:], ssum[:], 1.0 / D),
                    after=[t_sum, prev.get("nmr")])
        t_var = c.op("v", lambda v: v.tensor_scalar_mul(svar[:], ssq[:], 1.0 / D),
                     after=[t_sq, prev.get("rstduse")])
        t_mu2 = c.op("v", lambda v: v.tensor_mul(ssum[:], smu[:], smu[:]),
                     after=[t_mu])
        t_var2 = c.op("v", lambda v: v.tensor_sub(svar[:], svar[:], ssum[:]),
                      after=[t_var, t_mu2])
        t_ln = c.op("s", lambda s: s.activation(svar[:], svar[:], AF.Ln, bias=seps[:]),
                    after=[t_var2, t_eps])
        t_rstd = c.op("s", lambda s: s.activation(srstd[:], svar[:], AF.Exp, scale=-0.5),
                      after=[t_ln])
        t_nmr = c.op("v", lambda v: v.scalar_tensor_tensor(
            snmr[:], smu[:], -1.0, srstd[:], ALU.mult, ALU.mult),
            after=[t_rstd, t_mu, prev.get("apply_any")])
        t_apply = c.op("s", lambda s, b=b: s.activation(
            xnt[b][:], sx[b][:], AF.Identity, bias=snmr[:], scale=srstd[:]),
            after=[t_nmr, t_rstd, prev.get(("tr", b))])
        t_tr = None
        for ch in range(ND):
            t_tr = c.op("t", lambda t, b=b, ch=ch: t.transpose(
                ptr[b][:, ch * 128:(ch + 1) * 128],
                xnt[b][:, ch * 128:(ch + 1) * 128], ident[:]),
                after=([t_apply, prev.get(("evac", b))] if ch == 0 else []),
                inc=(1 if ch == ND - 1 else 0))
        t_evac = c.op("v", lambda v, b=b: v.tensor_copy(sxn[b][:], ptr[b][:]),
                      after=[t_tr, prev.get(("store", b))])
        t_store = None
        for ch in range(ND):
            t_store = c.op("q", lambda q, i=i, b=b, ch=ch: q.dma_start(
                out=xn_ap[ch * 128:(ch + 1) * 128, i * 128:(i + 1) * 128],
                in_=sxn[b][:, ch * 128:(ch + 1) * 128]),
                after=([t_evac] if ch == 0 else []), inc=16, sem=f"st{b}")
        prev[("apply", b)] = t_apply
        prev["apply_any"] = t_apply
        prev[("tr", b)] = t_tr
        prev[("evac", b)] = t_evac
        prev[("store", b)] = t_store
        prev["varuse"] = t_var
        prev["var2"] = t_var2
        prev["nmr"] = t_nmr
        prev["rstduse"] = t_rstd
    c.replay(bar)
    SES.close()


def st_proj0(nc, ES, bar, dr, tag="p0"):
    c = Ctx(nc, _POOL, tag)
    SES = contextlib.ExitStack()
    A = SES.enter_context
    xn_sb = [A(nc.sbuf_tensor(f"{tag}_xn{ch}", [128, T], BF16)) for ch in range(ND)]
    w_sb = {p: A(nc.sbuf_tensor(f"{tag}_w{p}", [128, ND * D], BF16)) for p in "qvk"}
    ob = [A(nc.sbuf_tensor(f"{tag}_ob{i}", [128, T], BF16)) for i in range(2)]
    kraw = A(nc.sbuf_tensor(f"{tag}_kraw", [128, T], F32))
    ksq = A(nc.sbuf_tensor(f"{tag}_ksq", [128, T], F32))
    ones_c = A(nc.sbuf_tensor(f"{tag}_onesc", [128, 1], F32))
    ones_r = A(nc.sbuf_tensor(f"{tag}_onesr", [65, 128], F32))
    nrmt = A(nc.sbuf_tensor(f"{tag}_nrmt", [65, T], F32))
    rstdt = A(nc.sbuf_tensor(f"{tag}_rstdt", [65, T], F32))

    ps = [A(nc.psum_tensor(f"{tag}_ps{i}", [128, 512], F32)) for i in range(4)]
    pna = A(nc.psum_tensor(f"{tag}_pna", [128, 512], F32))
    pnb = A(nc.psum_tensor(f"{tag}_pnb", [128, 512], F32))
    pb = A(nc.psum_tensor(f"{tag}_pb", [128, 512], F32))

    projs = ["q", "v", "k"]
    wname = {"q": "wqT", "v": "wvT", "k": "wkT"}
    oname = {"q": "q0", "v": "v0", "k": "k0"}

    t_ones = c.op("v", lambda v: v.memset(ones_c[:], 1.0))
    t_ones2 = c.op("v", lambda v: v.memset(ones_r[:], 1.0))
    tok_w = {}
    for p in projs:
        tw = None
        for ch in range(ND):
            tw = c.op("g", lambda g, p=p, ch=ch: g.dma_start(
                out=w_sb[p][:, ch * D:(ch + 1) * D],
                in_=dr[wname[p]][ch * 128:(ch + 1) * 128, :]), inc=16, sem=f"w{p}")
        tok_w[p] = tw
    t_xn = None
    for ch in range(ND):
        t_xn = c.op("q", lambda q, ch=ch: q.dma_start(
            out=xn_sb[ch][:], in_=dr["xn0"][ch * 128:(ch + 1) * 128, :]), inc=16, sem="lx")

    prev = {}
    for pi in range(24):
        ip, hp = divmod(pi, 8)
        p = projs[ip]
        b = pi % 2
        ev_toks = []
        for w in range(2):
            t_mm = None
            for ch in range(ND):
                for tc in range(4):
                    tch = w * 4 + tc
                    aft = []
                    if ch == 0 and tc == 0:
                        aft = [tok_w[p], t_xn, prev.get("wave_ev")]
                    t_mm = c.op("t", lambda t, p=p, ch=ch, hp=hp, tc=tc, tch=tch: t.matmul(
                        ps[tc][:],
                        w_sb[p][:, ch * D + hp * 128: ch * D + hp * 128 + 128],
                        xn_sb[ch][:, tch * 512:(tch + 1) * 512],
                        start=(ch == 0), stop=(ch == ND - 1)),
                        after=aft,
                        inc=(1 if (ch == ND - 1 and tc == 3) else 0))
            last_ev = None
            for tc in range(4):
                tch = w * 4 + tc
                sl = slice(tch * 512, (tch + 1) * 512)
                aft = [t_mm]
                if tc == 0:
                    if p == "k":
                        aft.append(prev.get("kraw_done"))
                    else:
                        aft.append(prev.get(("store", b)))
                if p == "q":
                    last_ev = c.op("v", lambda v, b=b, sl=sl, tc=tc: v.tensor_scalar_mul(
                        ob[b][:, sl], ps[tc][:], 0.1), after=aft)
                elif p == "v":
                    last_ev = c.op("v", lambda v, b=b, sl=sl, tc=tc: v.tensor_copy(
                        ob[b][:, sl], ps[tc][:]), after=aft)
                else:
                    last_ev = c.op("v", lambda v, sl=sl, tc=tc: v.tensor_copy(
                        kraw[:, sl], ps[tc][:]), after=aft)
            ev_toks.append(last_ev)
            prev["wave_ev"] = last_ev
        if p != "k":
            t_store = c.op("q", lambda q, p=p, hp=hp, b=b: q.dma_start(
                out=dr[oname[p]][hp * 128:(hp + 1) * 128, :], in_=ob[b][:]),
                after=[ev_toks[-1]], inc=16, sem=f"so{b}")
            prev[("store", b)] = t_store
        else:
            kj = hp
            t_sqr = c.op("s", lambda s: s.activation(ksq[:], kraw[:], AF.Square),
                         after=[ev_toks[-1], prev.get("ones_done")])
            t_on = None
            t_cp = prev.get("rowc")
            for tch in range(ND):
                aft = [t_sqr, t_ones] if tch == 0 else []
                aft.append(t_cp)
                c.op("t", lambda t, tch=tch: t.matmul(
                    pna[0:1, :], ones_c[0:64, :],
                    ksq[0:64, tch * 512:(tch + 1) * 512], start=True, stop=True),
                    after=aft, inc=0)
                t_on = c.op("t", lambda t, tch=tch: t.matmul(
                    pnb[0:1, :], ones_c[64:128, :],
                    ksq[64:128, tch * 512:(tch + 1) * 512], start=True, stop=True),
                    after=[], inc=1)
                sl = slice(tch * 512, (tch + 1) * 512)
                c.op("v", lambda v, sl=sl: v.tensor_copy(nrmt[0:1, sl], pna[0:1, :]),
                     after=[t_on, prev.get("lnuse") if tch == 0 else None])
                t_cp = c.op("v", lambda v, sl=sl: v.tensor_copy(nrmt[64:65, sl], pnb[0:1, :]),
                     after=[])
            prev["ones_done"] = t_on
            t_g0 = c.op("v", lambda v: v.tensor_scalar_max(nrmt[0:1, :], nrmt[0:1, :], 1e-24),
                        after=[t_cp])
            t_g1 = c.op("v", lambda v: v.tensor_scalar_max(nrmt[64:65, :], nrmt[64:65, :], 1e-24))
            prev["rowc"] = t_g1
            t_r0a = c.op("s", lambda s: s.activation(rstdt[0:1, :], nrmt[0:1, :], AF.Ln),
                         after=[t_g0, t_g1, prev.get("bcast_done")])
            t_r0 = c.op("s", lambda s: s.activation(rstdt[0:1, :], rstdt[0:1, :], AF.Exp, scale=-0.5),
                        after=[t_r0a])
            t_r1a = c.op("s", lambda s: s.activation(rstdt[64:65, :], nrmt[64:65, :], AF.Ln),
                         after=[])
            t_r1 = c.op("s", lambda s: s.activation(rstdt[64:65, :], rstdt[64:65, :], AF.Exp, scale=-0.5),
                        after=[t_r1a])
            prev["lnuse"] = t_r1
            t_norm = prev.get("normuse")
            t_bc = None
            for tch in range(ND):
                sl = slice(tch * 512, (tch + 1) * 512)
                aft0 = [t_r0, t_r1, t_ones2] if tch == 0 else []
                aft0.append(t_norm)
                c.op("t", lambda t, sl=sl: t.matmul(
                    pb[0:64, :], ones_r[0:1, 0:64], rstdt[0:1, sl],
                    start=True, stop=True), after=aft0, inc=0)
                c.op("t", lambda t: t.matmul(
                    pna[0:1, 480:496], ones_r[0:1, 0:1], ones_r[0:1, 0:16],
                    start=True, stop=True, skip_group_check=True), after=[], inc=0)
                t_bc = c.op("t", lambda t, sl=sl: t.matmul(
                    pb[64:128, :], ones_r[64:65, 0:64], rstdt[64:65, sl],
                    start=True, stop=True, tile_position=(64, 64)), after=[], inc=1)
                aft1 = [t_bc]
                if tch == 0:
                    aft1.append(prev.get(("store", b)))
                t_norm = c.op("v", lambda v, b=b, sl=sl: v.tensor_mul(
                    ob[b][:, sl], kraw[:, sl], pb[:]), after=aft1)
            prev["bcast_done"] = t_bc
            prev["normuse"] = t_norm
            prev["kraw_done"] = t_norm
            t_store = c.op("q", lambda q, hp=hp, b=b: q.dma_start(
                out=dr["k0"][hp * 128:(hp + 1) * 128, :], in_=ob[b][:]),
                after=[t_norm], inc=16, sem=f"so{b}")
            prev[("store", b)] = t_store
    c.replay(bar)
    SES.close()


def st_wo(nc, ES, bar, dr, mo_name, w_name, out_name, bias_name=None, tag="wo"):
    c = Ctx(nc, _POOL, tag)
    SES = contextlib.ExitStack()
    A = SES.enter_context
    mo_sb = [A(nc.sbuf_tensor(f"{tag}_mo{ch}", [128, T], BF16)) for ch in range(ND)]
    w_sb = A(nc.sbuf_tensor(f"{tag}_w", [128, ND * D], BF16))
    so = [A(nc.sbuf_tensor(f"{tag}_so{i}", [128, D], F32)) for i in range(2)]
    pso = [A(nc.psum_tensor(f"{tag}_ps{i}", [128, 512], F32)) for i in range(4)]
    bias_row = A(nc.sbuf_tensor(f"{tag}_brow", [1, D], F32)) if bias_name else None
    bias_bc = A(nc.sbuf_tensor(f"{tag}_bbc", [128, D], F32)) if bias_name else None
    ones_r = A(nc.sbuf_tensor(f"{tag}_onesr", [1, 128], F32)) if bias_name else None
    pbias = A(nc.psum_tensor(f"{tag}_pbias", [128, 512], F32)) if bias_name else None

    t_w = None
    for ch in range(ND):
        t_w = c.op("g", lambda g, ch=ch: g.dma_start(
            out=w_sb[:, ch * D:(ch + 1) * D],
            in_=dr[w_name][ch * 128:(ch + 1) * 128, :]), inc=16, sem="lw")
    t_b = None
    if bias_name:
        t_br = c.op("q", lambda q: q.dma_start(out=bias_row[:], in_=dr[bias_name]), inc=16, sem="lb")
        t_o = c.op("v", lambda v: v.memset(ones_r[:], 1.0))
        for half in range(2):
            t_bm = c.op("t", lambda t, half=half: t.matmul(
                pbias[:], ones_r[:], bias_row[0:1, half * 512:(half + 1) * 512],
                start=True, stop=True), after=[t_br, t_o, t_b], inc=1)
            t_b = c.op("v", lambda v, half=half: v.tensor_copy(
                bias_bc[:, half * 512:(half + 1) * 512], pbias[:]), after=[t_bm])
    t_mo = None
    for ch in range(ND):
        t_mo = c.op("q", lambda q, ch=ch: q.dma_start(
            out=mo_sb[ch][:], in_=dr[mo_name][ch * 128:(ch + 1) * 128, :]), inc=16, sem="lm")

    prev = {}
    for i in range(NT):
        b = i % 2
        t_mm = None
        for ch in range(ND):
            for half in range(2):
                aft = []
                if ch == 0 and half == 0:
                    aft = [t_w, t_mo, prev.get(("ev", b))]
                t_mm = c.op("t", lambda t, ch=ch, half=half, i=i, b=b: t.matmul(
                    pso[b * 2 + half][:],
                    mo_sb[ch][:, i * 128:(i + 1) * 128],
                    w_sb[:, ch * D + half * 512: ch * D + (half + 1) * 512],
                    start=(ch == 0), stop=(ch == ND - 1)),
                    after=aft, inc=(1 if (ch == ND - 1 and half == 1) else 0))
        t_ev = None
        for half in range(2):
            aft = [t_mm]
            if half == 0:
                aft.append(prev.get(("st", b)))
            if bias_name:
                aft.append(t_b)
                t_ev = c.op("v", lambda v, b=b, half=half: v.tensor_add(
                    so[b][:, half * 512:(half + 1) * 512], pso[b * 2 + half][:],
                    bias_bc[:, half * 512:(half + 1) * 512]), after=aft)
            else:
                t_ev = c.op("v", lambda v, b=b, half=half: v.tensor_copy(
                    so[b][:, half * 512:(half + 1) * 512], pso[b * 2 + half][:]),
                    after=aft)
        prev[("ev", b)] = t_ev
        t_st = c.op("q", lambda q, i=i, b=b: q.dma_start(
            out=dr[out_name][i * 128:(i + 1) * 128, :], in_=so[b][:]),
            after=[t_ev], inc=16, sem=f"st{b}")
        prev[("st", b)] = t_st
    c.replay(bar)
    SES.close()


def st_mixer0(nc, ES, bar, dr, tag="mx"):
    c = Ctx(nc, _POOL, tag)
    SES = contextlib.ExitStack()
    A = SES.enter_context
    NS = 32
    _npairs = int(os.environ.get("MXPAIRS", "8"))
    _ph = os.environ.get("MXPH", "abc")
    ident = dr["c_ident_sb"]
    lmask = dr["c_lmask_sb"]
    umask = dr["c_umask_sb"]
    mask0 = dr["c_mask0_sb"]

    pat = [A(nc.psum_tensor(f"{tag}_pat{i}", [128, 512], F32)) for i in range(2)]
    pbank2 = A(nc.psum_tensor(f"{tag}_pb2", [128, 512], F32))
    pbank3 = A(nc.psum_tensor(f"{tag}_pb3", [128, 512], F32))
    pbank4 = A(nc.psum_tensor(f"{tag}_pb4", [128, 512], F32))
    pbank5 = A(nc.psum_tensor(f"{tag}_pb5", [128, 512], F32))
    pbank6 = A(nc.psum_tensor(f"{tag}_pb6", [128, 1024], BF16))
    pbank7 = A(nc.psum_tensor(f"{tag}_pb7", [128, 1024], BF16))
    pgr = pbank2[:, 0:128]
    pks = pbank2[:, 128:256]
    pab = pbank2[:, 256:256 + H]
    pdum = pbank2[:, 288:304]
    ppw = pbank3[:, 0:128]
    pr = pbank3[:, 128:256]
    ppt = pbank4[:, 0:128]
    psu = pbank4[:, 128:192]
    psc = pbank5[:, 0:128]
    pm = pbank5[:, 128:192]
    ptr_bf = [pbank6[:, 0:128], pbank7[:, 0:128]]
    pprT = [pbank6[:, 128:256], pbank7[:, 128:256]]

    qp = A(nc.sbuf_tensor(f"{tag}_qp", [128, T], BF16))
    kp = A(nc.sbuf_tensor(f"{tag}_kp", [128, T], BF16))
    vp = A(nc.sbuf_tensor(f"{tag}_vp", [128, T], BF16))
    kap = A(nc.sbuf_tensor(f"{tag}_kap", [128, T], BF16))
    ktm = A(nc.sbuf_tensor(f"{tag}_ktm", [128, T], BF16))
    vtm = A(nc.sbuf_tensor(f"{tag}_vtm", [128, T], BF16))
    vbtm = A(nc.sbuf_tensor(f"{tag}_vbtm", [128, T], BF16))
    TTh = {0: A(nc.sbuf_tensor(f"{tag}_TTa", [128, NS * 128], BF16)),
           1: A(nc.sbuf_tensor(f"{tag}_TTb", [128, NS * 128], BF16))}
    probsT = {0: A(nc.sbuf_tensor(f"{tag}_pTa", [128, NS * 128], BF16)),
              1: A(nc.sbuf_tensor(f"{tag}_pTb", [128, NS * 128], BF16))}
    mo = [A(nc.sbuf_tensor(f"{tag}_mo{i}", [128, T], BF16)) for i in range(2)]
    XA = A(nc.sbuf_tensor(f"{tag}_XA", [128, 128], BF16))
    XAT = A(nc.sbuf_tensor(f"{tag}_XAT", [128, 128], BF16))
    XB = A(nc.sbuf_tensor(f"{tag}_XB", [128, 128], BF16))
    XBT = A(nc.sbuf_tensor(f"{tag}_XBT", [128, 128], BF16))
    PT = A(nc.sbuf_tensor(f"{tag}_PT", [128, 128], BF16))
    t1 = [A(nc.sbuf_tensor(f"{tag}_t1{i}", [128, 128], F32)) for i in range(2)]
    prb = [A(nc.sbuf_tensor(f"{tag}_prb{i}", [128, 128], BF16)) for i in range(2)]
    nmx = A(nc.sbuf_tensor(f"{tag}_nmx", [128, 1], F32))
    ssum = A(nc.sbuf_tensor(f"{tag}_ssum", [128, 1], F32))
    srec = A(nc.sbuf_tensor(f"{tag}_srec", [128, 1], F32))
    RHSb = A(nc.sbuf_tensor(f"{tag}_RHSb", [128, 128], BF16))
    Rb = A(nc.sbuf_tensor(f"{tag}_Rb", [128, 128], BF16))
    Sf = A(nc.sbuf_tensor(f"{tag}_Sf", [128, 64], F32))
    Sb = A(nc.sbuf_tensor(f"{tag}_Sb", [128, 128], BF16))
    Mf = A(nc.sbuf_tensor(f"{tag}_Mf", [128, 64], F32))
    Mb = A(nc.sbuf_tensor(f"{tag}_Mb", [128, 128], BF16))
    araw = A(nc.sbuf_tensor(f"{tag}_araw", [1, H], F32))
    braw = A(nc.sbuf_tensor(f"{tag}_braw", [1, H], F32))
    asig = A(nc.sbuf_tensor(f"{tag}_asig", [1, H], F32))
    bsig = A(nc.sbuf_tensor(f"{tag}_bsig", [1, H], F32))
    onesr = A(nc.sbuf_tensor(f"{tag}_onesr", [1, 128], F32))
    abc = A(nc.sbuf_tensor(f"{tag}_abc", [128, H], F32))
    anbc = A(nc.sbuf_tensor(f"{tag}_anbc", [128, H], F32))
    bbc = A(nc.sbuf_tensor(f"{tag}_bbc", [128, H], F32))
    apair = A(nc.sbuf_tensor(f"{tag}_apair", [128, 8], F32))

    rows = [slice(0, 64), slice(64, 128)]

    t_lar = c.op("q", lambda q: q.dma_start(out=araw[:], in_=dr["alpha_raw"]), inc=16, sem="la")
    t_lbr = c.op("q", lambda q: q.dma_start(out=braw[:], in_=dr["beta_raw"]), inc=16, sem="lb")
    t_or = c.op("v", lambda v: v.memset(onesr[:], 1.0))
    t_as = c.op("s", lambda s: s.activation(asig[:], araw[:], AF.Sigmoid), after=[t_lar])
    t_bs = c.op("s", lambda s: s.activation(bsig[:], braw[:], AF.Sigmoid), after=[t_lbr])
    t_abm = c.op("t", lambda t: t.matmul(pab, onesr[:], asig[:], start=True, stop=True),
                 after=[t_as, t_or])
    t_ab1 = c.op("v", lambda v: v.tensor_scalar_mul(abc[:], pab, 0.5), after=[t_abm])
    t_ab2 = c.op("v", lambda v: v.tensor_scalar_mul(anbc[:], pab, -0.5), after=[t_ab1])
    t_bbm = c.op("t", lambda t: t.matmul(pab, onesr[:], bsig[:], start=True, stop=True),
                 after=[t_bs, t_ab2])
    t_bb1 = c.op("v", lambda v: v.tensor_scalar_mul(bbc[:], pab, 2.0), after=[t_bbm])
    t_zb1 = c.op("v", lambda v: v.memset(Sb[:], 0.0))
    t_zb2 = c.op("v", lambda v: v.memset(Mb[:], 0.0), after=[t_zb1])
    t_apr = t_bb1
    for hp in range(8):
        t_apr = c.op("v", lambda v, hp=hp: v.tensor_copy(
            apair[0:64, hp:hp + 1], abc[0:64, 2 * hp:2 * hp + 1]), after=[t_apr])
        t_apr = c.op("v", lambda v, hp=hp: v.tensor_copy(
            apair[64:128, hp:hp + 1], abc[64:128, 2 * hp + 1:2 * hp + 2]), after=[t_apr])

    prevpair = {}
    for hp in range(_npairs):
        ha, hb = 2 * hp, 2 * hp + 1
        gate = prevpair.get("done", [])
        t_lq = c.op("q", lambda q, hp=hp: q.dma_start(
            out=qp[:], in_=dr["q0"][hp * 128:(hp + 1) * 128, :]), after=gate, inc=16, sem="lq")
        t_lk = c.op("q", lambda q, hp=hp: q.dma_start(
            out=kp[:], in_=dr["k0"][hp * 128:(hp + 1) * 128, :]), after=gate, inc=16, sem="lk")
        t_lv = c.op("q", lambda q, hp=hp: q.dma_start(
            out=vp[:], in_=dr["v0"][hp * 128:(hp + 1) * 128, :]), after=gate, inc=16, sem="lv")
        t_ka = c.op("v", lambda v, hp=hp: v.tensor_scalar(
            kap[:], kp[:], apair[:, hp:hp + 1], None, ALU.mult),
            after=[t_lk, t_apr] + gate)

        t_ktm = {}
        t_vtm = {}
        t_vbtm = {}
        if _ph == "a1":
            t_most = c.op("q", lambda q, hp=hp: q.dma_start(
                out=dr["mo0"][hp * 128:(hp + 1) * 128, :], in_=mo[hp % 2][:]),
                after=[t_ka], inc=16, sem=f"mo{hp % 2}")
            prevpair["done"] = [t_ka]
            prevpair[("most", hp % 2)] = t_most
            continue
        for s in range(NS):
            bb = s % 2
            aft = [t_lk]
            if s < 2:
                aft.append(prevpair.get("trfree"))
            else:
                aft.append(t_ktm.get(s - 2))
            c.op("t", lambda t, s=s, bb=bb: t.transpose(
                ptr_bf[bb][:, 0:64], kp[0:64, s * 128:(s + 1) * 128], ident[0:64, 0:64]),
                after=aft, inc=0)
            c.op("t", lambda t: t.matmul(pdum[0:1, 0:16], onesr[0:1, 0:1],
                                         onesr[0:1, 0:16], start=True, stop=True,
                                         skip_group_check=True), after=[t_or], inc=0)
            tk2 = c.op("t", lambda t, s=s, bb=bb: t.transpose(
                ptr_bf[bb][:, 64:128], kp[64:128, s * 128:(s + 1) * 128],
                ident[64:128, 64:128]), after=[], inc=1)
            t_ktm[s] = c.op("v", lambda v, s=s, bb=bb: v.tensor_copy(
                ktm[:, s * 128:(s + 1) * 128], ptr_bf[bb]), after=[tk2])
        if _ph == "a2":
            t_most = c.op("q", lambda q, hp=hp: q.dma_start(
                out=dr["mo0"][hp * 128:(hp + 1) * 128, :], in_=mo[hp % 2][:]),
                after=[t_ktm[NS - 1]], inc=16, sem=f"mo{hp % 2}")
            prevpair["done"] = [t_ktm[NS - 1]]
            prevpair["trfree"] = t_ktm[NS - 1]
            prevpair[("most", hp % 2)] = t_most
            continue
        for s in range(NS):
            bb = s % 2
            aft = [t_lv, t_ktm[NS - 2 + bb], t_vbtm.get(s - 2)]
            c.op("t", lambda t, s=s, bb=bb: t.transpose(
                ptr_bf[bb][:, 0:64], vp[0:64, s * 128:(s + 1) * 128], ident[0:64, 0:64]),
                after=aft, inc=0)
            c.op("t", lambda t: t.matmul(pdum[0:1, 0:16], onesr[0:1, 0:1],
                                         onesr[0:1, 0:16], start=True, stop=True,
                                         skip_group_check=True), after=[], inc=0)
            tv2 = c.op("t", lambda t, s=s, bb=bb: t.transpose(
                ptr_bf[bb][:, 64:128], vp[64:128, s * 128:(s + 1) * 128],
                ident[64:128, 64:128]), after=[], inc=1)
            t_vtm[s] = c.op("v", lambda v, s=s, bb=bb: v.tensor_copy(
                vtm[:, s * 128:(s + 1) * 128], ptr_bf[bb]), after=[tv2])
            c.op("v", lambda v, s=s, bb=bb, ha=ha: v.tensor_scalar(
                vbtm[:, s * 128:s * 128 + 64], ptr_bf[bb][:, 0:64],
                bbc[:, ha:ha + 1], None, ALU.mult), after=[t_vtm[s]])
            t_vbtm[s] = c.op("v", lambda v, s=s, bb=bb, hb=hb: v.tensor_scalar(
                vbtm[:, s * 128 + 64:(s + 1) * 128], ptr_bf[bb][:, 64:128],
                bbc[:, hb:hb + 1], None, ALU.mult), after=[])

        t_TT = {}
        t_prT = {}
        if "b" not in _ph:
            t_most = c.op("q", lambda q, hp=hp: q.dma_start(
                out=dr["mo0"][hp * 128:(hp + 1) * 128, :], in_=mo[hp % 2][:]),
                after=[t_vbtm[NS - 1]], inc=16, sem=f"mo{hp % 2}")
            prevpair["done"] = [t_vbtm[NS - 1]]
            prevpair["trfree"] = t_vbtm[NS - 1]
            prevpair[("most", hp % 2)] = t_most
            continue
        prev_ex = prevpair.get("ex")
        prev_rc = prevpair.get("rc")
        prev_pa = prevpair.get("pa")
        prev_XT = prevpair.get("lastXT")
        for hi in (0, 1):
            hh = 2 * hp + hi
            rsl = rows[hi]
            for s in range(NS):
                bb = s % 2
                csl = slice(s * 128, (s + 1) * 128)
                t_gr = c.op("t", lambda t, rsl=rsl, csl=csl: t.matmul(
                    pgr, kp[rsl, csl], kp[rsl, csl], start=True, stop=True),
                    after=[t_lk, t_TT.get((hi, s - 1)),
                           t_vbtm[NS - 1] if (hi == 0 and s == 0) else None,
                           t_TT.get((0, NS - 1)) if (hi == 1 and s == 0) else None,
                           prevpair.get("chainend") if (hi == 0 and s == 0) else None],
                    inc=1)
                t_X = c.op("v", lambda v, hh=hh: v.scalar_tensor_tensor(
                    XA[:], pgr, anbc[0:128, hh:hh + 1], lmask[:], ALU.mult, ALU.mult),
                    after=[t_gr, t_ab2, prev_pa, prev_XT])
                t_XT = c.op("v", lambda v, hh=hh: v.scalar_tensor_tensor(
                    XAT[:], pgr, anbc[0:128, hh:hh + 1], umask[:], ALU.mult, ALU.mult),
                    after=[t_X])
                t_PT = c.op("v", lambda v: v.tensor_add(PT[:], XAT[:], ident[:]),
                            after=[t_XT])
                cur = (XA, XAT)
                nxt = (XB, XBT)
                tokX, tokXT = t_X, t_XT
                t_pa = t_PT
                for lvl in range(6):
                    t_p1 = c.op("t", lambda t, cur=cur: t.matmul(
                        ppw, cur[1][:], cur[0][:], start=True, stop=True),
                        after=[tokX, tokXT, t_pa], inc=1)
                    t_e1 = c.op("v", lambda v, nxt=nxt: v.tensor_copy(nxt[0][:], ppw),
                                after=[t_p1, t_pa])
                    if lvl < 5:
                        t_p2 = c.op("t", lambda t, cur=cur: t.matmul(
                            ppt, cur[0][:], cur[1][:], start=True, stop=True),
                            after=[], inc=1)
                        t_e2 = c.op("v", lambda v, nxt=nxt: v.tensor_copy(nxt[1][:], ppt),
                                    after=[t_p2])
                    else:
                        t_e2 = t_e1
                    t_p3 = c.op("t", lambda t, nxt=nxt: t.matmul(
                        ppw, nxt[0][:], PT[:], start=True, stop=True),
                        after=[t_e1, t_e2], inc=1)
                    if lvl < 4:
                        t_pa = c.op("v", lambda v: v.tensor_add(PT[:], ppw, PT[:]),
                                    after=[t_p3])
                    else:
                        t_pa = c.op("v", lambda v, hi=hi, csl=csl: v.tensor_add(
                            TTh[hi][:, csl], ppw, PT[:]), after=[t_p3])
                    cur, nxt = nxt, cur
                    tokX, tokXT = t_e1, t_e2
                t_TT[(hi, s)] = t_pa
                prev_pa = t_pa
                prev_XT = t_XT
                t_sc = c.op("t", lambda t, rsl=rsl, csl=csl: t.matmul(
                    psc, qp[rsl, csl], kp[rsl, csl], start=True, stop=True),
                    after=[t_lq, t_prT.get((hi, s - 1)),
                           t_prT.get((0, NS - 1)) if (hi == 1 and s == 0) else None,
                           prevpair.get("chainend") if (hi == 0 and s == 0) else None],
                    inc=1)
                t_m = c.op("v", lambda v, bb=bb: v.scalar_tensor_tensor(
                    t1[bb][:], psc, 1.25, mask0[:], ALU.mult, ALU.add),
                    after=[t_sc, prev_ex])
                t_mx = c.op("v", lambda v, bb=bb: v.reduce_max(
                    nmx[:], t1[bb][:], axis=AX.X, negate=True), after=[t_m, prev_ex])
                t_ex = c.op("s", lambda s_, bb=bb: s_.activation(
                    prb[bb][:], t1[bb][:], AF.Exp, bias=nmx[:], accum_out=ssum[:]),
                    after=[t_mx, prev_rc])
                t_rc = c.op("v", lambda v: v.reciprocal(srec[:], ssum[:]), after=[t_ex])
                t_nm = c.op("v", lambda v, bb=bb: v.tensor_scalar(
                    prb[bb][:], prb[bb][:], srec[:, 0:1], None, ALU.mult), after=[t_rc])
                t_tr = c.op("t", lambda t, bb=bb: t.transpose(
                    pprT[bb], prb[bb][:], ident[:]),
                    after=[t_nm, t_prT.get((hi, s - 2)),
                           t_prT.get((0, NS - 2 + bb)) if (hi == 1 and s < 2) else None,
                           prevpair.get("prfree") if (hi == 0 and s < 2) else None],
                    inc=1)
                t_prT[(hi, s)] = c.op("v", lambda v, bb=bb, hi=hi, csl=csl: v.tensor_copy(
                    probsT[hi][:, csl], pprT[bb]), after=[t_tr])
                prev_ex = t_ex
                prev_rc = t_rc

        if "c" not in _ph:
            t_most = c.op("q", lambda q, hp=hp: q.dma_start(
                out=dr["mo0"][hp * 128:(hp + 1) * 128, :], in_=mo[hp % 2][:]),
                after=[t_prT[(1, NS - 1)]], inc=16, sem=f"mo{hp % 2}")
            prevpair["done"] = [t_prT[(1, NS - 1)]]
            prevpair["chainend"] = t_prT[(1, NS - 1)]
            prevpair["ex"] = t_ex
            prevpair["rc"] = t_rc
            prevpair["pa"] = t_pa
            prevpair["lastXT"] = t_XT
            prevpair["trfree"] = t_vbtm[NS - 1]
            prevpair["prfree"] = t_prT[(1, NS - 1)]
            prevpair[("most", hp % 2)] = t_most
            prevpair["pvuse"] = None
            continue
        chaingate = [t_TT[(1, NS - 1)], t_prT[(1, NS - 1)], prevpair.get("chainend")]
        t_z1 = c.op("v", lambda v: v.memset(Sf[:], 0.0), after=chaingate + [t_zb2])
        t_z2 = c.op("v", lambda v: v.memset(Sb[0:64, 0:64], 0.0), after=[t_z1])
        t_z2 = c.op("v", lambda v: v.memset(Sb[64:128, 64:128], 0.0), after=[t_z2])
        t_z3 = c.op("v", lambda v: v.memset(Mf[:], 0.0), after=[t_z2])
        t_z4 = c.op("v", lambda v: v.memset(Mb[0:64, 0:64], 0.0), after=[t_z3])
        t_z4 = c.op("v", lambda v: v.memset(Mb[64:128, 64:128], 0.0), after=[t_z4])
        tSb, tMb, tSf, tMf = t_z4, t_z4, t_z4, t_z4
        t_rhs = None
        t_rev = None
        t_moev = {}
        mb = hp % 2
        for s in range(NS):
            bb = s % 2
            csl = slice(s * 128, (s + 1) * 128)
            t_ks = c.op("t", lambda t, csl=csl: t.matmul(
                pks, kap[:, csl], Sb[:],
                start=True, stop=True, skip_group_check=True),
                after=[t_ka, tSb, t_z4] + chaingate + [t_rhs], inc=1)
            t_rhs = c.op("v", lambda v, csl=csl: v.scalar_tensor_tensor(
                RHSb[:], vbtm[:, csl], 1.0, pks, ALU.mult, ALU.subtract),
                after=[t_ks, t_vbtm[NS - 1], t_rev])
            t_r = None
            for hi in (0, 1):
                t_r = c.op("t", lambda t, hi=hi, csl=csl: t.matmul(
                    pr[:, 64 * hi:64 * hi + 64], TTh[hi][:, csl],
                    RHSb[:, 64 * hi:64 * hi + 64], start=True, stop=True, skip_group_check=True),
                    after=[t_rhs, t_rev] if hi == 0 else [], inc=(1 if hi == 1 else 0))
            t_rev = c.op("v", lambda v: v.tensor_copy(Rb[:], pr), after=[t_r])
            t_su = None
            for hi in (0, 1):
                t_su = c.op("t", lambda t, hi=hi, csl=csl: t.matmul(
                    psu[64 * hi:64 * hi + 64, :], ktm[:, csl][:, 64 * hi:64 * hi + 64],
                    Rb[:, 64 * hi:64 * hi + 64], start=True, stop=True, skip_group_check=True,
                    tile_position=(0, 64 * hi)),
                    after=[t_rev, tSf, t_ktm[NS - 1]] if hi == 0 else [],
                    inc=(1 if hi == 1 else 0))
            tSf = c.op("v", lambda v: v.tensor_add(Sf[:], psu, Sf[:]),
                       after=[t_su, tSb])
            tSb = c.op("v", lambda v: v.tensor_copy(Sb[0:64, 0:64], Sf[0:64, :]), after=[tSf])
            tSb = c.op("v", lambda v: v.tensor_copy(Sb[64:128, 64:128], Sf[64:128, :]),
                       after=[tSb])
            pa = pat[bb]
            for hi in (0, 1):
                t_av = c.op("t", lambda t, hi=hi, csl=csl, pa=pa: t.matmul(
                    pa[64 * hi:64 * hi + 64, 0:128], vtm[:, csl][:, 64 * hi:64 * hi + 64],
                    probsT[hi][:, csl], start=True, stop=True, skip_group_check=True,
                    tile_position=(0, 64 * hi)),
                    after=[t_vtm[NS - 1], t_moev.get(s - 2),
                           prevpair.get(("moev", bb)) if s < 2 else None] if hi == 0 else [],
                    inc=(1 if hi == 1 else 0))
            t_m1 = c.op("t", lambda t, csl=csl, pa=pa: t.matmul(
                pa[:, 128:192], Mb[:], qp[:, csl][:, 0:64],
                start=True, stop=True, skip_group_check=True),
                after=[tMb], inc=0)
            t_md = None
            for hi in (0, 1):
                t_md = c.op("t", lambda t, hi=hi, csl=csl: t.matmul(
                    pm[64 * hi:64 * hi + 64, :], Rb[0:64, 64 * hi:64 * hi + 64],
                    ktm[0:64, csl][:, 64 * hi:64 * hi + 64], start=True, stop=True, skip_group_check=True,
                    tile_position=(0, 64 * hi)),
                    after=[t_rev, tMf] if hi == 0 else [], inc=(1 if hi == 1 else 0))
            tMf = c.op("v", lambda v: v.tensor_add(Mf[:], pm, Mf[:]),
                       after=[t_md, tMb])
            tMb = c.op("v", lambda v: v.tensor_copy(Mb[0:64, 0:64], Mf[0:64, :]), after=[tMf])
            tMb = c.op("v", lambda v: v.tensor_copy(Mb[64:128, 64:128], Mf[64:128, :]),
                       after=[tMb])
            t_m2 = c.op("t", lambda t, csl=csl, pa=pa: t.matmul(
                pa[:, 192:256], Mb[:], qp[:, csl][:, 64:128],
                start=True, stop=True, skip_group_check=True),
                after=[tMb], inc=1)
            t_me = None
            for hi in (0, 1):
                t_me = c.op("t", lambda t, hi=hi, csl=csl: t.matmul(
                    pm[64 * hi:64 * hi + 64, :], Rb[64:128, 64 * hi:64 * hi + 64],
                    ktm[64:128, csl][:, 64 * hi:64 * hi + 64], start=True, stop=True, skip_group_check=True,
                    tile_position=(64, 64 * hi)),
                    after=[tMf, tMb] if hi == 0 else [], inc=(1 if hi == 1 else 0))
            tMf = c.op("v", lambda v: v.tensor_add(Mf[:], pm, Mf[:]), after=[t_me])
            tMb = c.op("v", lambda v: v.tensor_copy(Mb[0:64, 0:64], Mf[0:64, :]), after=[tMf])
            tMb = c.op("v", lambda v: v.tensor_copy(Mb[64:128, 64:128], Mf[64:128, :]),
                       after=[tMb])
            t_mo1 = c.op("v", lambda v, csl=csl, pa=pa, mb=mb: v.tensor_copy(
                mo[mb][:, csl], pa[:, 0:128]),
                after=[t_av, t_m2, prevpair.get(("most", mb)) if s == 0 else None])
            t_moev[s] = c.op("v", lambda v, csl=csl, pa=pa, mb=mb: v.tensor_add(
                mo[mb][:, csl], pa[:, 128:256], mo[mb][:, csl]),
                after=[t_mo1])
        t_most = c.op("q", lambda q, hp=hp, mb=mb: q.dma_start(
            out=dr["mo0"][hp * 128:(hp + 1) * 128, :], in_=mo[mb][:]),
            after=[t_moev[NS - 1]], inc=16, sem=f"mo{mb}")
        prevpair["most", mb] = t_most
        prevpair["done"] = [t_moev[NS - 1], tMb]
        prevpair["chainend"] = tMb
        prevpair["ex"] = t_ex
        prevpair["rc"] = t_rc
        prevpair["pa"] = t_pa
        prevpair["lastXT"] = t_XT
        prevpair["trfree"] = t_vbtm[NS - 1]
        prevpair["prfree"] = t_prT[(1, NS - 1)]
        prevpair[("moev", 0)] = t_moev[NS - 2]
        prevpair[("moev", 1)] = t_moev[NS - 1]
    c.replay(bar)
    SES.close()


def st_ffn0(nc, ES, bar, dr, tag="f0"):
    c = Ctx(nc, _POOL, tag)
    SES = contextlib.ExitStack()
    A = SES.enter_context
    xn_sb = [A(nc.sbuf_tensor(f"{tag}_xn{ch}", [128, T], BF16)) for ch in range(ND)]
    keys_sb = A(nc.sbuf_tensor(f"{tag}_keys", [128, ND * E * 128], BF16))
    vals_sb = A(nc.sbuf_tensor(f"{tag}_vals", [128, E * D], BF16))
    es_sb = A(nc.sbuf_tensor(f"{tag}_es", [128, ND * E], BF16))
    self_fm = A(nc.sbuf_tensor(f"{tag}_selfm", [65, T], F32))
    selb = A(nc.sbuf_tensor(f"{tag}_selb", [128, 256], F32))
    hbf = A(nc.sbuf_tensor(f"{tag}_hbf", [128, 256], BF16))
    sx = [A(nc.sbuf_tensor(f"{tag}_sx{i}", [128, D], F32)) for i in range(2)]
    ss = [A(nc.sbuf_tensor(f"{tag}_ss{i}", [128, D], F32)) for i in range(2)]
    sg = [A(nc.sbuf_tensor(f"{tag}_sg{i}", [128, D], F32)) for i in range(2)]
    so = [A(nc.sbuf_tensor(f"{tag}_so{i}", [128, D], F32)) for i in range(2)]
    pg = [A(nc.psum_tensor(f"{tag}_pg{i}", [128, 512], F32)) for i in range(4)]
    ph = A(nc.psum_tensor(f"{tag}_ph", [128, 512], F32))
    pb = A(nc.psum_tensor(f"{tag}_pb", [128, 512], F32))
    psel = A(nc.psum_tensor(f"{tag}_psel", [128, 512], F32))

    selfm = self_fm

    t_k = None
    for ch in range(ND):
        for e in range(E):
            t_k = c.op("g", lambda g, ch=ch, e=e: g.dma_start(
                out=keys_sb[:, ch * E * 128 + e * 128: ch * E * 128 + (e + 1) * 128],
                in_=dr["keys"][e * D + ch * 128: e * D + (ch + 1) * 128, :]),
                inc=16, sem="lk")
    t_v = None
    for e in range(E):
        t_v = c.op("g", lambda g, e=e: g.dma_start(
            out=vals_sb[:, e * D:(e + 1) * D],
            in_=dr["values"][e * 128:(e + 1) * 128, :]), inc=16, sem="lv")
    t_e = None
    for ch in range(ND):
        t_e = c.op("g", lambda g, ch=ch: g.dma_start(
            out=es_sb[:, ch * E:(ch + 1) * E],
            in_=dr["esT"][ch * 128:(ch + 1) * 128, :]), inc=16, sem="le")
    t_xn = None
    for ch in range(ND):
        t_xn = c.op("q", lambda q, ch=ch: q.dma_start(
            out=xn_sb[ch][:], in_=dr["xn0"][ch * 128:(ch + 1) * 128, :]), inc=16, sem="lx")

    t_self = {}
    for tc in range(8):
        t_mm = None
        for ch in range(ND):
            t_mm = c.op("t", lambda t, ch=ch, tc=tc: t.matmul(
                psel[0:16, :], es_sb[:, ch * E:(ch + 1) * E],
                xn_sb[ch][:, tc * 512:(tc + 1) * 512],
                start=(ch == 0), stop=(ch == ND - 1)),
                after=[t_e, t_xn, t_self.get(tc - 1)] if ch == 0 else [],
                inc=(1 if ch == ND - 1 else 0))
        t_self[tc] = c.op("s", lambda s, tc=tc: s.activation(
            selfm[0:16, tc * 512:(tc + 1) * 512], psel[0:16, :], AF.Sigmoid),
            after=[t_mm, t_self.get(tc - 1)])
    t_selfin = t_self[7]

    prev = {}
    NSUP = 16
    for sp in range(NSUP):
        t0 = sp * 256
        bb = sp % 2
        t_hev = prev.get("hev")
        t_gmm_last = None
        t_glast = prev.get("guse")
        for e in range(E):
            t_hmm = None
            for ch in range(ND):
                t_hmm = c.op("t", lambda t, ch=ch, e=e, t0=t0: t.matmul(
                    ph[:, 0:256],
                    keys_sb[:, ch * E * 128 + e * 128: ch * E * 128 + (e + 1) * 128],
                    xn_sb[ch][:, t0:t0 + 256],
                    start=(ch == 0), stop=(ch == ND - 1)),
                    after=[t_k, prev.get("huse")] if ch == 0 else [],
                    inc=(1 if ch == ND - 1 else 0))
            t_bc = c.op("t", lambda t, e=e, t0=t0: t.matmul(
                pb[:, 0:256], dr["c_oh_sb"][:, e * 128:(e + 1) * 128],
                selfm[0:16, t0:t0 + 256],
                start=True, stop=True), after=[t_selfin, prev.get("selbuse")], inc=1)
            t_sb = c.op("v", lambda v: v.tensor_copy(selb[:, 0:256], pb[:, 0:256]),
                        after=[t_bc, prev.get("hmul")])
            prev["selbuse"] = t_sb
            t_hm = c.op("v", lambda v: v.scalar_tensor_tensor(
                hbf[:, 0:256], ph[:, 0:256], 0.0, selb[:, 0:256], ALU.max, ALU.mult),
                after=[t_hmm, t_sb, t_glast])
            prev["hmul"] = t_hm
            prev["huse"] = t_hm
            for tt in range(2):
                for half in range(2):
                    t_gmm_last = c.op("t", lambda t, e=e, tt=tt, half=half: t.matmul(
                        pg[tt * 2 + half][:],
                        hbf[:, tt * 128:(tt + 1) * 128],
                        vals_sb[:, e * D + half * 512: e * D + (half + 1) * 512],
                        start=(e == 0), stop=(e == E - 1)),
                        after=[t_hm, t_v, prev.get("gev") if e == 0 and tt == 0 and half == 0 else None],
                        inc=(1 if (tt == 1 and half == 1) else 0))
            t_glast = t_gmm_last
        prev["guse"] = t_gmm_last
        t_gev = None
        for tt in range(2):
            i = sp * 2 + tt
            b2 = i % 2
            t_lx = c.op("q", lambda q, i=i, b2=b2: q.dma_start(
                out=sx[b2][:], in_=dr["x0"][i * 128:(i + 1) * 128, :]),
                after=[prev.get(("cmb", b2))], inc=16, sem=f"lx{b2}")
            t_ls = c.op("q", lambda q, i=i, b2=b2: q.dma_start(
                out=ss[b2][:], in_=dr["s0"][i * 128:(i + 1) * 128, :]),
                after=[prev.get(("cmb", b2))], inc=16, sem=f"ls{b2}")
            t_sil = None
            for half in range(2):
                t_sil = c.op("s", lambda s, tt=tt, half=half, b2=b2: s.activation(
                    sg[b2][:, half * 512:(half + 1) * 512], pg[tt * 2 + half][:], AF.Silu),
                    after=[t_gmm_last, prev.get(("sguse", b2))] if half == 0 else [])
            t_c1 = c.op("v", lambda v, b2=b2: v.tensor_mul(sg[b2][:], sg[b2][:], ss[b2][:]),
                        after=[t_sil, t_ls, prev.get(("st", b2))])
            t_c2 = c.op("v", lambda v, b2=b2: v.tensor_add(so[b2][:], sg[b2][:], sx[b2][:]),
                        after=[t_c1, t_lx])
            prev[("sguse", b2)] = t_c1
            prev[("cmb", b2)] = t_c2
            t_gev = t_c2
            t_st = c.op("q", lambda q, i=i, b2=b2: q.dma_start(
                out=dr["x1"][i * 128:(i + 1) * 128, :], in_=so[b2][:]),
                after=[t_c2], inc=16, sem=f"st{b2}")
            prev[("st", b2)] = t_st
        prev["gev"] = t_gev
        prev["hev"] = t_gev
    c.replay(bar)
    SES.close()


def st_attn1(nc, ES, bar, dr, tag="a1"):
    c = Ctx(nc, _POOL, tag)
    SES = contextlib.ExitStack()
    A = SES.enter_context
    ident = dr["c_ident_sb"]
    corr = dr["c_corr_sb"]

    xn_sb = [A(nc.sbuf_tensor(f"{tag}_xn{ch}", [128, T], BF16)) for ch in range(ND)]
    w_sb = A(nc.sbuf_tensor(f"{tag}_w", [128, ND * 3 * D], BF16))
    qaug = A(nc.sbuf_tensor(f"{tag}_qaug", [88, T], BF16))
    klow = A(nc.sbuf_tensor(f"{tag}_klow", [88, T], BF16))
    khigh = A(nc.sbuf_tensor(f"{tag}_khigh", [88, T], BF16))
    cl88 = A(nc.sbuf_tensor(f"{tag}_cl88", [88, T], BF16))
    vsb = A(nc.sbuf_tensor(f"{tag}_vsb", [64, T], BF16))
    vtm = A(nc.sbuf_tensor(f"{tag}_vtm", [128, 32 * 65], BF16))
    moh = A(nc.sbuf_tensor(f"{tag}_moh", [64, T], BF16))
    prT = [A(nc.sbuf_tensor(f"{tag}_prT{i}", [128, 2048], BF16)) for i in range(2)]
    qsq = A(nc.sbuf_tensor(f"{tag}_qsq", [64, 512], F32))
    bq = A(nc.sbuf_tensor(f"{tag}_bq", [128, 24], F32))
    s8 = A(nc.sbuf_tensor(f"{tag}_s8", [1, H], F32))
    s8bc = A(nc.sbuf_tensor(f"{tag}_s8bc", [128, H], F32))
    s8nbc = A(nc.sbuf_tensor(f"{tag}_s8nbc", [128, H], F32))
    onesc = A(nc.sbuf_tensor(f"{tag}_onesc", [128, 1], F32))
    onesr = A(nc.sbuf_tensor(f"{tag}_onesr", [1, 128], F32))
    rowmx = A(nc.sbuf_tensor(f"{tag}_rowmx", [1, 512], F32))
    mq = A(nc.sbuf_tensor(f"{tag}_mq", [1, 1], F32))
    mk = A(nc.sbuf_tensor(f"{tag}_mk", [1, 1], F32))
    mtot = A(nc.sbuf_tensor(f"{tag}_mtot", [1, 1], F32))
    negm = A(nc.sbuf_tensor(f"{tag}_negm", [128, 1], F32))
    srec = A(nc.sbuf_tensor(f"{tag}_srec", [1, 512], F32))
    rb_sb = A(nc.sbuf_tensor(f"{tag}_rb", [64, 512], F32))
    psc4 = A(nc.psum_tensor(f"{tag}_psc4", [128, 2048], F32))
    pn = A(nc.psum_tensor(f"{tag}_pn", [128, 512], F32))
    pv0 = A(nc.psum_tensor(f"{tag}_pv0", [65, 512], F32))
    pv = [pv0, pv0]
    ptr = A(nc.psum_tensor(f"{tag}_ptr", [128, 1024], BF16))
    ptr2 = A(nc.psum_tensor(f"{tag}_ptr2", [128, 1024], BF16))
    ptrs = [ptr, ptr2]

    t_w = None
    for ch in range(ND):
        t_w = c.op("g", lambda g, ch=ch: g.dma_start(
            out=w_sb[:, ch * 3 * D:(ch + 1) * 3 * D],
            in_=dr["wqkv"][ch * 128:(ch + 1) * 128, :]), inc=16, sem="lw")
    t_xn = None
    for ch in range(ND):
        t_xn = c.op("q", lambda q, ch=ch: q.dma_start(
            out=xn_sb[ch][:], in_=dr["xn1"][ch * 128:(ch + 1) * 128, :]), inc=16, sem="lx")
    t_cq = c.op("q", lambda q: q.dma_start(out=qaug[64:88, :], in_=dr["c_qaug"]), inc=16, sem="lc1")
    t_cl = c.op("q", lambda q: q.dma_start(out=cl88[64:88, :], in_=dr["c_klow"]), inc=16, sem="lc2")
    t_bq = c.op("q", lambda q: q.dma_start(out=bq[:], in_=dr["bqkv"]), inc=16, sem="lb1")
    t_s8 = c.op("q", lambda q: q.dma_start(out=s8[:], in_=dr["c_slope8"]), inc=16, sem="lb2")
    t_on = c.op("v", lambda v: v.memset(onesc[:], 1.0))
    t_on2 = c.op("v", lambda v: v.memset(onesr[:], 1.0))
    t_vone = None
    for j in range(NT):
        t_vone = c.op("v", lambda v, j=j: v.memset(vtm[:, j * 65 + 64:j * 65 + 65], 1.0),
                      after=[t_vone])
    t_qc = t_cq
    t_s8m = c.op("t", lambda t: t.matmul(pn[:, 0:H], onesr[:], s8[:], start=True, stop=True),
                 after=[t_s8, t_on2], inc=1)
    t_s8b = c.op("v", lambda v: v.tensor_copy(s8bc[:], pn[:, 0:H]), after=[t_s8m])
    t_s8n = c.op("v", lambda v: v.tensor_scalar_mul(s8nbc[:], pn[:, 0:H], -1.0), after=[t_s8b])

    prev = {}
    for h in range(H):
        heven = (h % 2 == 0)
        brow = (slice(0, 64) if heven else slice(64, 128))
        bcol = h // 2
        toks = {}
        for pi, (dst, dsl, col0) in enumerate([
                (qaug, slice(0, 64), h * 64),
                (klow, slice(0, 64), D + h * 64),
                (vsb, slice(0, 64), 2 * D + h * 64)]):
            for wv in range(2):
                t_mm = None
                for ch in range(ND):
                    for tc4 in range(4):
                        tc = wv * 4 + tc4
                        aft = []
                        if ch == 0 and tc4 == 0:
                            aft = [t_w, t_xn, prev.get("scuse"), toks.get((pi, wv - 1)),
                                   toks.get((pi - 1, 1)) if wv == 0 else None]
                            if pi == 0 and wv == 0:
                                aft.append(prev.get("lastexp"))
                        t_mm = c.op("t", lambda t, ch=ch, tc=tc, tc4=tc4, col0=col0: t.matmul(
                            psc4[0:64, tc4 * 512:(tc4 + 1) * 512],
                            w_sb[:, ch * 3 * D + col0: ch * 3 * D + col0 + 64],
                            xn_sb[ch][:, tc * 512:(tc + 1) * 512],
                            start=(ch == 0), stop=(ch == ND - 1), skip_group_check=True),
                            after=aft, inc=(1 if (ch == ND - 1 and tc4 == 3) else 0))
                ev = None
                for tc4 in range(4):
                    tc = wv * 4 + tc4
                    ev = c.op("v", lambda v, dst=dst, dsl=dsl, tc=tc, tc4=tc4, brow=brow, bcol=bcol, pi=pi: v.tensor_scalar(
                        dst[dsl, tc * 512:(tc + 1) * 512],
                        psc4[0:64, tc4 * 512:(tc4 + 1) * 512],
                        bq[brow, (pi * 8 + bcol):(pi * 8 + bcol) + 1],
                        None, ALU.add),
                        after=[t_mm, t_bq] if tc4 == 0 else [])
                toks[(pi, wv)] = ev
        t_qev = toks[(0, 1)]
        t_kev = toks[(1, 1)]
        t_vev = toks[(2, 1)]
        t_kcp = c.op("v", lambda v: v.tensor_copy(khigh[0:64, :], klow[0:64, :]),
                     after=[t_kev, prev.get("scuse2")])
        t_kl24 = c.op("v", lambda v, h=h: v.tensor_scalar(
            klow[64:88, :], cl88[64:88, :], s8bc[64:88, h:h + 1], None, ALU.mult),
            after=[t_cl, t_s8b, t_kev])
        t_kh24 = c.op("v", lambda v, h=h: v.tensor_scalar(
            khigh[64:88, :], cl88[64:88, :], s8nbc[64:88, h:h + 1], None, ALU.mult),
            after=[t_cl, t_s8n, t_kcp])
        t_vt = {}
        for j in range(NT):
            aft = [t_vev, t_vone]
            aft.append(prev.get("vtuse") if j < 2 else t_vt.get(j - 2))
            ttr = c.op("t", lambda t, j=j: t.transpose(
                ptrs[j % 2][:, 0:64], vsb[0:64, j * 128:(j + 1) * 128], ident[0:64, 0:64]),
                after=aft, inc=1)
            t_vt[j] = c.op("v", lambda v, j=j: v.tensor_copy(
                vtm[:, j * 65:j * 65 + 64], ptrs[j % 2][:, 0:64]), after=[ttr])
        t_vtl = t_vt[NT - 1]
        t_mq = None
        t_mk = None
        t_nm = None
        t_prev = None
        for (src_t, tev, macc) in ((qaug, t_qev, "mq"), (klow, t_kev, "mk")):
            for tc in range(8):
                t_sq = c.op("s", lambda s, src_t=src_t, tc=tc: s.activation(
                    qsq[0:64, :], src_t[0:64, tc * 512:(tc + 1) * 512], AF.Square),
                    after=[tev, t_nm, prev.get("rmxuse") if (src_t is qaug and tc == 0) else None])
                t_nm = c.op("t", lambda t: t.matmul(
                    pn[0:1, :], onesc[0:64, :], qsq[0:64, :],
                    start=True, stop=True),
                    after=[t_sq, t_on, t_prev,
                           prev.get("rmxuse") if (src_t is qaug and tc == 0) else None],
                    inc=1)
                t_rx = c.op("v", lambda v: v.reduce_max(rowmx[0:1, 0:1], pn[0:1, :], axis=AX.X),
                            after=[t_nm, t_prev])
                if tc == 0:
                    t_prev = c.op("v", lambda v, macc=macc: v.tensor_copy(
                        (mq if macc == "mq" else mk)[:], rowmx[0:1, 0:1]), after=[t_rx])
                else:
                    t_prev = c.op("v", lambda v, macc=macc: v.tensor_max(
                        (mq if macc == "mq" else mk)[:], (mq if macc == "mq" else mk)[:],
                        rowmx[0:1, 0:1]), after=[t_rx])
            if src_t is qaug:
                t_mq = t_prev
            else:
                t_mk = t_prev
        t_sqk = t_mk
        t_mt = c.op("v", lambda v: v.tensor_add(mtot[:], mq[:], mk[:]), after=[t_mq, t_mk])
        t_mt2 = c.op("v", lambda v: v.tensor_scalar_mul(mtot[:], mtot[:], -1.0 / 16.0),
                     after=[t_mt])
        t_nbm = c.op("t", lambda t: t.matmul(pn[0:128, 0:1], onesr[:], mtot[:],
                                             start=True, stop=True),
                     after=[t_mt2], inc=1)
        t_ngb = c.op("v", lambda v: v.tensor_copy(negm[:], pn[0:128, 0:1]), after=[t_nbm])
        prev["nrmuse"] = t_sqk

        t_pvev = {}
        t_exp = {}
        for ic in range(8):
            bb = ic % 2
            i0 = ic * 512
            t_pvmm = None
            for jg in range(8):
                jlist = [jg * 4 + u for u in range(4)]
                t_smm = None
                for u, j in enumerate(jlist):
                    osl = psc4[:, u * 512:(u + 1) * 512]
                    jt0 = j * 128
                    if jt0 + 128 <= i0:
                        variants = [(klow, 0, 512, False)]
                    elif jt0 >= i0 + 512:
                        variants = [(khigh, 0, 512, False)]
                    else:
                        split = jt0 - i0
                        variants = []
                        if split > 0:
                            variants.append((khigh, 0, split, False))
                        variants.append((klow, split, 512 - split, True))
                    aft0 = [t_kl24, t_kh24, t_qc,
                            (t_exp.get((ic, jg - 1)) or t_exp.get((ic - 1, 7)) or prev.get("lastexp")) if u == 0 else None,
                            toks[(2, 1)] if (ic == 0 and jg == 0 and u == 0) else None]
                    for (ktile, cst, clen, isdiag) in variants:
                        t_smm = c.op("t", lambda t, ktile=ktile, j=j, cst=cst, clen=clen, u=u, i0=i0: t.matmul(
                            psc4[:, u * 512 + cst: u * 512 + cst + clen],
                            ktile[:, j * 128:(j + 1) * 128],
                            qaug[:, i0 + cst: i0 + cst + clen],
                            start=True, stop=True, skip_group_check=True),
                            after=aft0, inc=1)
                        aft0 = []
                t_corr = None
                for u, j in enumerate(jlist):
                    jt0 = j * 128
                    if i0 <= jt0 < i0 + 512:
                        off = jt0 - i0
                        t_corr = c.op("v", lambda v, u=u, off=off, h=h: v.scalar_tensor_tensor(
                            psc4[:, u * 512 + off: u * 512 + off + 128],
                            corr[:], s8nbc[:, h:h + 1],
                            psc4[:, u * 512 + off: u * 512 + off + 128],
                            ALU.mult, ALU.add),
                            after=[t_smm])
                prtguard = (t_pvmm if jg > 0 else
                            (t_pvev.get((ic - 2, 7)) if ic >= 2 else prev.get("pvuse")))
                t_exp[(ic, jg)] = c.op("s", lambda s, bb=bb: s.activation(
                    prT[bb][:], psc4[:], AF.Exp, bias=negm[:], scale=0.125),
                    after=[t_smm, t_corr, t_ngb, prtguard])
                for u, j in enumerate(jlist):
                    t_pvmm = c.op("t", lambda t, j=j, u=u, bb=bb: t.matmul(
                        pv[bb][:], vtm[:, j * 65:j * 65 + 65],
                        prT[bb][:, u * 512:(u + 1) * 512],
                        start=(j == 0), stop=(j == NT - 1), skip_group_check=True),
                        after=[t_exp[(ic, jg)], t_vtl,
                               (t_pvev.get((ic - 1, "fin")) or prev.get("pvevuse")) if (jg == 0 and u == 0) else None],
                        inc=(1 if u == 3 else 0))
                t_pvev[(ic, jg)] = t_pvmm
            t_rc = c.op("v", lambda v, bb=bb: v.reciprocal(srec[:], pv[bb][64:65, :]),
                        after=[t_pvmm, prev.get("srecuse")])
            t_bcm = c.op("t", lambda t, bb=bb: t.matmul(
                pn[0:64, :], onesr[0:1, 0:64], srec[:], start=True, stop=True),
                after=[t_rc], inc=1)
            t_rbe = c.op("v", lambda v: v.tensor_copy(rb_sb[:], pn[0:64, :]), after=[t_bcm])
            t_mul = c.op("v", lambda v, bb=bb, i0=i0: v.tensor_mul(
                moh[0:64, i0:i0 + 512], pv[bb][0:64, :], rb_sb[:]),
                after=[t_rbe, prev.get(("most",)) if ic < 2 else None])
            prev["srecuse"] = t_bcm
            t_pvev[(ic, "fin")] = t_mul
            prev["rmxuse"] = t_rbe
        prev["lastexp"] = t_exp[(7, 7)]
        prev["pvuse"] = t_pvev[(7, 7)]
        prev["scuse"] = t_exp[(7, 7)]
        prev["scuse2"] = t_exp[(7, 7)]
        prev["vtuse"] = t_pvev[(7, 7)]
        prev["pvevuse"] = t_pvev[(7, "fin")]
        t_st = c.op("q", lambda q, h=h: q.dma_start(
            out=dr["mo1"][h * 64:(h + 1) * 64, :], in_=moh[0:64, :]),
            after=[t_pvev[(7, "fin")]], inc=16, sem=f"mo{h % 2}")
        prev[("most",)] = t_st
    c.replay(bar)
    SES.close()


def st_gate1(nc, ES, bar, dr, tag="g1"):
    c = Ctx(nc, _POOL, tag)
    SES = contextlib.ExitStack()
    A = SES.enter_context
    ident = dr["c_ident_sb"]
    xn_sb = [A(nc.sbuf_tensor(f"{tag}_xn{ch}", [128, T], BF16)) for ch in range(ND)]
    sf1 = A(nc.sbuf_tensor(f"{tag}_sf1", [128, ND * 2048], BF16))
    sf2 = A(nc.sbuf_tensor(f"{tag}_sf2", [128, ND * 2048], BF16))
    sf3 = A(nc.sbuf_tensor(f"{tag}_sf3", [128, 16 * D], BF16))
    gw = A(nc.sbuf_tensor(f"{tag}_gw", [128, ND * E], BF16))
    usb = A(nc.sbuf_tensor(f"{tag}_usb", [128, 16 * 512], BF16))
    s1s = A(nc.sbuf_tensor(f"{tag}_s1s", [128, 512], BF16))
    lgt = A(nc.sbuf_tensor(f"{tag}_lgt", [128, E], F32))
    m1 = A(nc.sbuf_tensor(f"{tag}_m1", [128, 1], F32))
    m2 = A(nc.sbuf_tensor(f"{tag}_m2", [128, 1], F32))
    p1 = A(nc.sbuf_tensor(f"{tag}_p1", [128, 1], F32))
    p2 = A(nc.sbuf_tensor(f"{tag}_p2", [128, 1], F32))
    eq1 = A(nc.sbuf_tensor(f"{tag}_eq1", [128, E], F32))
    eq2 = A(nc.sbuf_tensor(f"{tag}_eq2", [128, E], F32))
    dsp = A(nc.sbuf_tensor(f"{tag}_dsp", [128, E], F32))
    dstg = A(nc.sbuf_tensor(f"{tag}_dstg", [16, 128], F32))
    shtg = [A(nc.sbuf_tensor(f"{tag}_sh{i}", [128, D], F32)) for i in range(2)]
    ps1 = A(nc.psum_tensor(f"{tag}_ps1", [128, 512], F32))
    ps2 = A(nc.psum_tensor(f"{tag}_ps2", [128, 512], F32))
    pou = A(nc.psum_tensor(f"{tag}_pou", [128, 1024], F32))
    plg = A(nc.psum_tensor(f"{tag}_plg", [128, 512], F32))

    t_w1 = None
    for ch in range(ND):
        t_w1 = c.op("g", lambda g, ch=ch: g.dma_start(
            out=sf1[:, ch * 2048:(ch + 1) * 2048],
            in_=dr["sfc1T"][ch * 128:(ch + 1) * 128, :]), inc=16, sem="l1")
    t_w2 = None
    for ch in range(ND):
        t_w2 = c.op("g", lambda g, ch=ch: g.dma_start(
            out=sf2[:, ch * 2048:(ch + 1) * 2048],
            in_=dr["sfc2T"][ch * 128:(ch + 1) * 128, :]), inc=16, sem="l2")
    t_w3 = None
    for ch in range(16):
        t_w3 = c.op("g", lambda g, ch=ch: g.dma_start(
            out=sf3[:, ch * D:(ch + 1) * D],
            in_=dr["sfc3T"][ch * 128:(ch + 1) * 128, :]), inc=16, sem="l3")
    t_gw = None
    for ch in range(ND):
        t_gw = c.op("g", lambda g, ch=ch: g.dma_start(
            out=gw[:, ch * E:(ch + 1) * E],
            in_=dr["gwT"][ch * 128:(ch + 1) * 128, :]), inc=16, sem="l4")
    t_xn = None
    for ch in range(ND):
        t_xn = c.op("q", lambda q, ch=ch: q.dma_start(
            out=xn_sb[ch][:], in_=dr["xn1"][ch * 128:(ch + 1) * 128, :]), inc=16, sem="lx")

    prev = {}
    for i in range(NT):
        t_mm = None
        for ch in range(ND):
            t_mm = c.op("t", lambda t, ch=ch, i=i: t.matmul(
                plg[:, 256:256 + E], xn_sb[ch][:, i * 128:(i + 1) * 128],
                gw[:, ch * E:(ch + 1) * E],
                start=(ch == 0), stop=(ch == ND - 1)),
                after=[t_gw, t_xn, prev.get("lgev")] if ch == 0 else [],
                inc=(1 if ch == ND - 1 else 0))
        t_lg = c.op("v", lambda v: v.tensor_copy(lgt[:], plg[:, 256:256 + E]),
                    after=[t_mm, prev.get("dspev")])
        prev["lgev"] = t_lg
        t_m1 = c.op("v", lambda v: v.reduce_max(m1[:], lgt[:], axis=AX.X),
                    after=[t_lg, prev.get("eq1use")])
        t_e1 = c.op("v", lambda v: v.tensor_scalar(eq1[:], lgt[:], m1[:, 0:1], None,
                                                   ALU.is_ge), after=[t_m1, prev.get("e1w")])
        t_l2 = c.op("v", lambda v: v.scalar_tensor_tensor(
            eq2[:], eq1[:], -1e30, lgt[:], ALU.mult, ALU.add), after=[t_e1])
        t_m2 = c.op("v", lambda v: v.reduce_max(m2[:], eq2[:], axis=AX.X), after=[t_l2])
        t_e2 = c.op("v", lambda v: v.tensor_scalar(eq2[:], eq2[:], m2[:, 0:1], None,
                                                   ALU.is_ge), after=[t_m2])
        t_dm = c.op("v", lambda v: v.tensor_sub(p1[:], m2[:], m1[:]), after=[t_m2])
        t_p1 = c.op("s", lambda s: s.activation(p2[:], p1[:], AF.Sigmoid),
                    after=[t_dm, prev.get("p2use")])
        t_p2 = c.op("v", lambda v: v.tensor_scalar(p1[:], p2[:], -1.0, 1.0,
                                                   ALU.mult, ALU.add), after=[t_p1])
        t_d1 = c.op("v", lambda v: v.tensor_scalar(dsp[:], eq1[:], p1[:, 0:1], None,
                                                   ALU.mult), after=[t_p2, prev.get("dspev")])
        t_d2 = c.op("v", lambda v: v.scalar_tensor_tensor(
            dsp[:], eq2[:], p2[:, 0:1], dsp[:], ALU.mult, ALU.add), after=[t_d1, t_e2])
        prev["e1w"] = t_d2
        prev["eq1use"] = t_d2
        prev["p2use"] = t_p2
        t_tr = c.op("t", lambda t: t.transpose(plg[0:16, 0:128], dsp[:], dr["c_identf_sb"][:]),
                    after=[t_d2, prev.get("trev")], inc=1)
        t_te = c.op("v", lambda v: v.tensor_copy(dstg[:], plg[0:16, 0:128]),
                    after=[t_tr, prev.get("dst")])
        prev["trev"] = t_te
        t_ds = c.op("q", lambda q, i=i: q.dma_start(
            out=dr["disp"][:, i * 128:(i + 1) * 128], in_=dstg[:]),
            after=[t_te], inc=16, sem="sd")
        prev["dst"] = t_ds
        prev["dspev"] = t_te
    for tc in range(8):
        t_ulast = prev.get("ulast")
        for sch in range(16):
            t_s1 = None
            for ch in range(ND):
                t_s1 = c.op("t", lambda t, ch=ch, sch=sch, tc=tc: t.matmul(
                    ps1[:], sf1[:, ch * 2048 + sch * 128: ch * 2048 + sch * 128 + 128],
                    xn_sb[ch][:, tc * 512:(tc + 1) * 512],
                    start=(ch == 0), stop=(ch == ND - 1)),
                    after=[t_w1, t_xn, prev.get("s1use")] if ch == 0 else [],
                    inc=(1 if ch == ND - 1 else 0))
            t_sil = c.op("s", lambda s: s.activation(s1s[:], ps1[:], AF.Silu),
                         after=[t_s1, prev.get("s1suse")])
            prev["s1use"] = t_sil
            t_s2 = None
            for ch in range(ND):
                t_s2 = c.op("t", lambda t, ch=ch, sch=sch, tc=tc: t.matmul(
                    ps2[:], sf2[:, ch * 2048 + sch * 128: ch * 2048 + sch * 128 + 128],
                    xn_sb[ch][:, tc * 512:(tc + 1) * 512],
                    start=(ch == 0), stop=(ch == ND - 1)),
                    after=[t_w2, prev.get("s2use")] if ch == 0 else [],
                    inc=(1 if ch == ND - 1 else 0))
            t_u = c.op("v", lambda v, sch=sch: v.scalar_tensor_tensor(
                usb[:, sch * 512:(sch + 1) * 512], ps2[:], 1.0, s1s[:],
                ALU.mult, ALU.mult), after=[t_s2, t_sil, t_ulast])
            prev["s2use"] = t_u
            prev["s1suse"] = t_u
        for ts in range(4):
            i = tc * 4 + ts
            b = i % 2
            t_om = None
            for sch in range(16):
                for half in range(2):
                    aft = []
                    if sch == 0 and half == 0:
                        aft = [t_w3, t_u, prev.get("pouuse")]
                    t_om = c.op("t", lambda t, sch=sch, half=half, ts=ts: t.matmul(
                        pou[:, half * 512:(half + 1) * 512],
                        usb[:, sch * 512 + ts * 128: sch * 512 + ts * 128 + 128],
                        sf3[:, sch * D + half * 512: sch * D + (half + 1) * 512],
                        start=(sch == 0), stop=(sch == 15), skip_group_check=True),
                        after=aft, inc=(1 if (sch == 15 and half == 1) else 0))
            t_se = c.op("v", lambda v, b=b: v.tensor_copy(shtg[b][:], pou[:]),
                        after=[t_om, prev.get(("shst2", b))])
            t_ss = c.op("q", lambda q, i=i, b=b: q.dma_start(
                out=dr["shared1"][i * 128:(i + 1) * 128, :], in_=shtg[b][:]),
                after=[t_se], inc=16, sem=f"sh{b}")
            prev["pouuse"] = t_se
            prev[("shst2", b)] = t_ss
        prev["ulast"] = t_se
    c.replay(bar)
    SES.close()


def st_hmoe(nc, ES, bar, dr, tag="h1"):
    c = Ctx(nc, _POOL, tag)
    SES = contextlib.ExitStack()
    A = SES.enter_context
    xn_sb = [A(nc.sbuf_tensor(f"{tag}_xn{ch}", [128, T], BF16)) for ch in range(ND)]
    w1e = [A(nc.sbuf_tensor(f"{tag}_w1e{i}", [128, ND * 512], BF16)) for i in range(2)]
    dsb = A(nc.sbuf_tensor(f"{tag}_dsb", [16, T], F32))
    selb = A(nc.sbuf_tensor(f"{tag}_selb", [128, 512], F32))
    hsil = A(nc.sbuf_tensor(f"{tag}_hsil", [128, 512], F32))
    hbf = [A(nc.sbuf_tensor(f"{tag}_hbf{i}", [128, 512], BF16)) for i in range(2)]
    ph = [A(nc.psum_tensor(f"{tag}_ph{i}", [128, 512], F32)) for i in range(4)]
    pb = A(nc.psum_tensor(f"{tag}_pb", [128, 512], F32))

    t_xn = None
    for ch in range(ND):
        t_xn = c.op("q", lambda q, ch=ch: q.dma_start(
            out=xn_sb[ch][:], in_=dr["xn1"][ch * 128:(ch + 1) * 128, :]), inc=16, sem="lx")
    t_d = c.op("q", lambda q: q.dma_start(out=dsb[:], in_=dr["disp"]), inc=16, sem="ld")

    prev = {}
    t_wl = {}
    for e in range(E):
        eb = e % 2
        t_w = None
        for ch in range(ND):
            t_w = c.op("g", lambda g, e=e, ch=ch, eb=eb: g.dma_start(
                out=w1e[eb][:, ch * 512:(ch + 1) * 512],
                in_=dr["w1T"][e * D + ch * 128: e * D + (ch + 1) * 128, :]),
                after=[prev.get(("wuse", eb))] if ch == 0 else [], inc=16, sem=f"lw{eb}")
        t_wl[e] = t_w
        for tc in range(8):
            hb = tc % 2
            t_bc = c.op("t", lambda t, e=e, tc=tc: t.matmul(
                pb[:], dr["c_oh_sb"][:, e * 128:(e + 1) * 128],
                dsb[0:16, tc * 512:(tc + 1) * 512], start=True, stop=True),
                after=[t_d, prev.get("selbuse")], inc=1)
            t_sb = c.op("v", lambda v: v.tensor_copy(selb[:], pb[:]),
                        after=[t_bc, prev.get("hmul")])
            prev["selbuse"] = t_sb
            t_hl = None
            for hch in range(4):
                t_hm = None
                for ch in range(ND):
                    t_hm = c.op("t", lambda t, ch=ch, hch=hch, tc=tc, eb=eb: t.matmul(
                        ph[hch][:], w1e[eb][:, ch * 512 + hch * 128: ch * 512 + hch * 128 + 128],
                        xn_sb[ch][:, tc * 512:(tc + 1) * 512],
                        start=(ch == 0), stop=(ch == ND - 1)),
                        after=[t_w, t_xn, prev.get(("phuse", hch))] if ch == 0 else [],
                        inc=(1 if ch == ND - 1 else 0))
                t_si = c.op("s", lambda s, hch=hch: s.activation(hsil[:], ph[hch][:], AF.Silu),
                            after=[t_hm, prev.get("hsiluse")])
                t_mu = c.op("v", lambda v, hb=hb: v.tensor_mul(hbf[hb][:], hsil[:], selb[:]),
                            after=[t_si, t_sb, prev.get(("hst", hb))])
                prev["hsiluse"] = t_mu
                prev["hmul"] = t_mu
                prev[("phuse", hch)] = t_mu
                t_st = c.op("q", lambda q, e=e, hch=hch, tc=tc, hb=hb: q.dma_start(
                    out=dr["hmoe"][e * 512 + hch * 128: e * 512 + (hch + 1) * 128,
                                   tc * 512:(tc + 1) * 512],
                    in_=hbf[hb][:]), after=[t_mu], inc=16, sem=f"hs{hb}")
                prev[("hst", hb)] = t_st
                t_hl = t_mu
        prev[("wuse", eb)] = t_hl
    c.replay(bar)
    SES.close()


def st_moe2(nc, ES, bar, dr, tag="m2"):
    c = Ctx(nc, _POOL, tag)
    SES = contextlib.ExitStack()
    A = SES.enter_context
    NEH = 64
    hm = A(nc.sbuf_tensor(f"{tag}_hm", [128, NEH * 512], BF16))
    w2b = [A(nc.sbuf_tensor(f"{tag}_w2b{i}", [128, D], BF16)) for i in range(2)]
    ssh = A(nc.sbuf_tensor(f"{tag}_ssh", [128, D], F32))
    ss1 = A(nc.sbuf_tensor(f"{tag}_ss1", [128, D], F32))
    sx1 = A(nc.sbuf_tensor(f"{tag}_sx1", [128, D], F32))
    sg = A(nc.sbuf_tensor(f"{tag}_sg", [128, D], F32))
    so = A(nc.sbuf_tensor(f"{tag}_so", [128, D], F32))
    pacc = [A(nc.psum_tensor(f"{tag}_pa{i}", [128, 1024], F32)) for i in range(4)]

    prev = {}
    for sup in range(8):
        t0 = sup * 512
        t_hm = c.op("q", lambda q, t0=t0: q.dma_start(
            out=hm[:],
            in_=dr["hmoe"].rearrange("(c p) t -> p c t", p=128)[:, :, t0:t0 + 512]),
            after=[prev.get("hmuse")], inc=16, sem="lh")
        t_mm_last = None
        for eh in range(NEH):
            eb = eh % 2
            t_w = c.op("g", lambda g, eh=eh, eb=eb: g.dma_start(
                out=w2b[eb][:], in_=dr["w2T"][eh * 128:(eh + 1) * 128, :]),
                after=[prev.get(("w2use", eb))], inc=16, sem=f"lw{eb}")
            for tt in range(4):
                for half in range(2):
                    aft = []
                    if tt == 0 and half == 0:
                        aft = [t_w, t_hm]
                        if eh == 0:
                            aft.append(prev.get("accuse"))
                    t_mm_last = c.op("t", lambda t, eh=eh, tt=tt, half=half, eb=eb: t.matmul(
                        pacc[tt][:, half * 512:(half + 1) * 512],
                        hm[:, eh * 512 + tt * 128: eh * 512 + tt * 128 + 128],
                        w2b[eb][:, half * 512:(half + 1) * 512],
                        start=(eh == 0), stop=(eh == NEH - 1), skip_group_check=True),
                        after=aft, inc=(1 if (tt == 3 and half == 1) else 0))
            prev[("w2use", eb)] = t_mm_last
        prev["hmuse"] = t_mm_last
        t_acc_done = t_mm_last
        t_cmb = None
        for tt in range(4):
            i = sup * 4 + tt
            t_lsh = c.op("q", lambda q, i=i: q.dma_start(
                out=ssh[:], in_=dr["shared1"][i * 128:(i + 1) * 128, :]),
                after=[prev.get("cmb1")], inc=16, sem="lsh")
            t_ls1 = c.op("q", lambda q, i=i: q.dma_start(
                out=ss1[:], in_=dr["s1"][i * 128:(i + 1) * 128, :]),
                after=[prev.get("cmb2")], inc=16, sem="ls1")
            t_lx1 = c.op("q", lambda q, i=i: q.dma_start(
                out=sx1[:], in_=dr["x1"][i * 128:(i + 1) * 128, :]),
                after=[prev.get("cmb3")], inc=16, sem="lx1")
            t_c0 = c.op("v", lambda v, tt=tt: v.tensor_add(sg[:], pacc[tt][:], ssh[:]),
                        after=[t_acc_done, t_lsh, prev.get("cmb4")])
            t_c1 = c.op("s", lambda s: s.activation(sg[:], sg[:], AF.Silu), after=[t_c0])
            t_c2 = c.op("v", lambda v: v.tensor_mul(sg[:], sg[:], ss1[:]),
                        after=[t_c1, t_ls1])
            t_c3 = c.op("v", lambda v: v.tensor_add(so[:], sg[:], sx1[:]),
                        after=[t_c2, t_lx1, prev.get("outst")])
            t_st = c.op("q", lambda q, i=i: q.dma_start(
                out=dr["out"][i * 128:(i + 1) * 128, :], in_=so[:]),
                after=[t_c3], inc=16, sem="so")
            prev["cmb1"] = t_c0
            prev["cmb2"] = t_c2
            prev["cmb3"] = t_c3
            prev["cmb4"] = t_c2
            prev["outst"] = t_st
            t_cmb = t_c0
        prev["accuse"] = prev["cmb1"]
    c.replay(bar)
    SES.close()


def _stage_registry():
    return {
        "ln0": (lambda nc, ES, bar, dr: st_ln(nc, ES, bar, dr, "x0", "xn0", "ln0"),
                ["x0"], ["xn0"]),
        "proj0": (st_proj0, ["xn0"], ["q0", "k0", "v0"]),
        "mixer0": (st_mixer0, ["q0", "k0", "v0"], ["mo0"]),
        "wo0": (lambda nc, ES, bar, dr: st_wo(nc, ES, bar, dr, "mo0", "woT", "s0", None, "wo0"),
                ["mo0"], ["s0"]),
        "ffn0": (st_ffn0, ["xn0", "x0", "s0"], ["x1"]),
        "ln1": (lambda nc, ES, bar, dr: st_ln(nc, ES, bar, dr, "x1", "xn1", "ln1"),
                ["x1"], ["xn1"]),
        "attn1": (st_attn1, ["xn1"], ["mo1"]),
        "wout1": (lambda nc, ES, bar, dr: st_wo(nc, ES, bar, dr, "mo1", "wout", "s1", "bout", "wout1"),
                  ["mo1"], ["s1"]),
        "gate1": (st_gate1, ["xn1"], ["disp", "shared1"]),
        "hmoe1": (st_hmoe, ["xn1", "disp"], ["hmoe"]),
        "moe2": (st_moe2, ["hmoe", "shared1", "s1", "x1"], ["out"]),
    }


def build_program(stage_names):
    reg = _stage_registry()
    full = set(stage_names) >= set(reg.keys())
    produced, consumed = set(), set()
    for s in stage_names:
        _, ins, outs = reg[s]
        consumed.update(ins)
        produced.update(outs)
    produced.discard("out")
    has_out = any("out" in reg[s][2] for s in stage_names)
    ext_in, ext_out = [], []
    nc = bass.Bass("TRN2", target_bir_lowering=False)
    dr = {}
    if "x0" in consumed or full:
        dr["x0"] = nc.dram_tensor("x0", [T, D], F32, kind="ExternalInput").ap()
        ext_in.append("x0")
    for name, (shp, dt) in {**WEIGHT_SHAPES, **CONST_SHAPES}.items():
        dr[name] = nc.dram_tensor(name, shp, dt, kind="ExternalInput").ap()
        ext_in.append(name)
    for name, (shp, dt) in INTER_SHAPES.items():
        if name in produced:
            kind = "Internal" if full else "ExternalOutput"
            dr[name] = nc.dram_tensor(name, shp, dt, kind=kind).ap()
            if kind == "ExternalOutput":
                ext_out.append(name)
        elif name in consumed:
            dr[name] = nc.dram_tensor(name, shp, dt, kind="ExternalInput").ap()
            ext_in.append(name)
    if has_out:
        dr["out"] = nc.dram_tensor("out", [T, D], F32, kind="ExternalOutput").ap()
        ext_out.append("out")

    ES = contextlib.ExitStack()
    with ES:
        global _POOL
        _POOL = SemPool(nc, ES, n=40)
        bar = {"sem": ES.enter_context(nc.semaphore("bar")), "n": 0}
        st_setup(nc, ES, bar, dr)
        for s in stage_names:
            fn, _, _ = reg[s]
            fn(nc, ES, bar, dr)
    return nc, ext_in, ext_out


def run_stages(stage_names, per_core_inputs, n_cores=8, **runkw):
    nc, ext_in, ext_out = build_program(stage_names)
    in_maps = [{k: per_core_inputs[cc][k] for k in ext_in} for cc in range(n_cores)]
    res = run_bass_kernel_spmd(nc, in_maps, core_ids=list(range(n_cores)), **runkw)
    return res


def kernel(**inputs):
    consts = host_constants()
    w = host_prep(inputs)
    x = np.ascontiguousarray(np.asarray(inputs["L0_attn_x"], dtype=np.float32))
    reg = _stage_registry()
    names = list(reg.keys())
    per_core = []
    for cc in range(8):
        m = dict(consts)
        m.update(w)
        m["x0"] = np.ascontiguousarray(x[cc])
        per_core.append(m)
    res = run_stages(names, per_core)
    return np.stack([r["out"] for r in res.results], axis=0)
